# version 11 (final)
# baseline (speedup 1.0000x reference)
import contextlib
import numpy as np
import ml_dtypes
import concourse.bass as bass
import concourse.mybir as mybir
from concourse.bass_utils import run_bass_kernel_spmd

F32 = mybir.dt.float32
BF16 = mybir.dt.bfloat16
AF = mybir.ActivationFunctionType
ALU = mybir.AluOpType
AX = mybir.AxisListType
BF = ml_dtypes.bfloat16

D = 2048
AD = 512
KD = 16
ADT = 4
C = 128
E = 1024
ET = 8
NU = 8
HT = 16
TB = 512
EPS = 1e-5


class Sems:
    def __init__(self, nc, stack, names):
        self.h = {}
        self.c = {}
        for n in names:
            self.h[n] = stack.enter_context(nc.semaphore(n))
            self.c[n] = 0

    def dma(self, eng, dst, src, name):
        eng.dma_start(dst, src).then_inc(self.h[name], 16)
        self.c[name] += 16
        return self.c[name]

    def inc(self, inst, name):
        inst.then_inc(self.h[name], 1)
        self.c[name] += 1
        return self.c[name]

    def flush(self, eng, inst, name="fl"):
        inst.then_inc(self.h[name], 1)
        self.c[name] += 1
        eng.wait_ge(self.h[name], self.c[name])


def rep_ap(t, offset_elems, pstride, reps, inner):
    h = t.tensor if hasattr(t, "tensor") else t
    return bass.AP(h, offset_elems, [[pstride, 128], [0, reps], [1, inner]])


ABLOCKS = None

def build_nc(T, dbg=False, extk=()):
    def KINDF(name):
        if dbg or name in extk:
            return dict(kind="ExternalOutput")
        return {}
    NCH = T // C
    NB = T // TB
    nc = bass.Bass("TRN2", target_bir_lowering=False)

    xT = nc.dram_tensor("xT", [D, T], F32, kind="ExternalInput")
    wq = nc.dram_tensor("wq", [D, AD], F32, kind="ExternalInput")
    wk = nc.dram_tensor("wk", [3, D, AD], F32, kind="ExternalInput")
    wv = nc.dram_tensor("wv", [3, D, AD], F32, kind="ExternalInput")
    wo = nc.dram_tensor("wo", [AD, D], F32, kind="ExternalInput")
    wg3h = nc.dram_tensor("wg3h", [128, 48], F32, kind="ExternalInput")
    dlog = nc.dram_tensor("dlog", [1, 3], F32, kind="ExternalInput")
    oscale = nc.dram_tensor("oscale", [1, 1], F32, kind="ExternalInput")
    fgh = nc.dram_tensor("fgh", [2, 128, 128], F32, kind="ExternalInput")
    bgh = nc.dram_tensor("bgh", [1, 16], F32, kind="ExternalInput")
    fwup = nc.dram_tensor("fwup", [6, D, E], F32, kind="ExternalInput")
    fwsw = nc.dram_tensor("fwsw", [6, E, E], F32, kind="ExternalInput")
    fwdn = nc.dram_tensor("fwdn", [6, E, D], F32, kind="ExternalInput")
    fwsu = nc.dram_tensor("fwsu", [D, D], F32, kind="ExternalInput")
    fwsa = nc.dram_tensor("fwsa", [E, E], F32, kind="ExternalInput")
    fwsd = nc.dram_tensor("fwsd", [D, D], F32, kind="ExternalInput")
    biascols = nc.dram_tensor("biascols", [128, 120], F32, kind="ExternalInput")
    bexth = nc.dram_tensor("bexth", [7, D], F32, kind="ExternalInput")
    eye_bf_d = nc.dram_tensor("eye_bfd", [128, 128], BF16, kind="ExternalInput")
    eye_f_d = nc.dram_tensor("eye_fd", [128, 128], F32, kind="ExternalInput")
    ones1f_d = nc.dram_tensor("ones1fd", [1, 128], F32, kind="ExternalInput")
    onescol_d = nc.dram_tensor("onescold", [128, 1], BF16, kind="ExternalInput")
    selbf_d = nc.dram_tensor("selbfd", [8, 768], BF16, kind="ExternalInput")
    ubits_d = nc.dram_tensor("ubitsd", [128, 7], F32, kind="ExternalInput")
    c1bits_d = nc.dram_tensor("c1bitsd", [128, 7], F32, kind="ExternalInput")
    expnbits_d = nc.dram_tensor("expnbitsd", [128, 896], BF16, kind="ExternalInput")
    causT_d = nc.dram_tensor("causTd", [128, 128], BF16, kind="ExternalInput")

    y_s = nc.dram_tensor("y_s", [D, T], F32, **KINDF("y_s"))
    r0hi_s = nc.dram_tensor("r0hi_s", [D, T], BF16, **KINDF("r0hi_s"))
    r0lo_s = nc.dram_tensor("r0lo_s", [D, T], BF16, **KINDF("r0lo_s"))
    xt1hi_s = nc.dram_tensor("xt1hi_s", [D, T], BF16, **KINDF("xt1hi_s"))
    xt1lo_s = nc.dram_tensor("xt1lo_s", [D, T], BF16, **KINDF("xt1lo_s"))
    wqkv_s = nc.dram_tensor("wqkv_s", [7, 128, KD, AD], BF16)
    wup_s = nc.dram_tensor("wup_s", [NU, 8, 128, KD, 128], BF16)
    wsw_s = nc.dram_tensor("wsw_s", [NU, 8, 128, ET, 128], BF16)
    wdn_s = nc.dram_tensor("wdn_s", [HT, 128, NU, ET, 128], BF16)
    x2T = nc.dram_tensor("x2T", [D, T], F32, kind="ExternalOutput")
    dbgout = nc.dram_tensor("dbgout", [128, 32768], F32, kind="ExternalOutput") if dbg else None
    dbgb = nc.dram_tensor("dbgb", [128, 8192], BF16, kind="ExternalOutput") if dbg else None

    def fm(dr):
        return dr.rearrange("(k p) t -> p k t", p=128)

    with contextlib.ExitStack() as top:
        S = Sems(nc, top, ["ld", "st", "xld", "wld", "dld", "mm", "ev", "evv", "prog", "fl"])
        sb = lambda name, shape, dt: top.enter_context(nc.sbuf_tensor(name, shape, dt))

        def mk_rec():
            Q = {"tensor": [], "scalar": [], "vector": [], "sync": [], "gpsimd": []}
            TRACE = {k: [] for k in Q}
            if not hasattr(nc, "_dbg_traces"):
                nc._dbg_traces = []
            nc._dbg_traces.append((dict(S.c), TRACE))

            def em(eng, fn):
                Q[eng].append(fn)

            def wv(eng, sem, cnt):
                assert cnt <= S.c[sem], (eng, sem, cnt, S.c[sem])
                TRACE[eng].append(("wait", sem, cnt))
                em(eng, lambda e, sem=sem, cnt=cnt: e.wait_ge(S.h[sem], cnt))

            def wr(eng, rel):
                if rel is not None:
                    wv(eng, rel[0], rel[1])

            def dmaQ(eng, sem, dst, src):
                S.c[sem] += 16
                TRACE[eng].append(("inc", sem, 16))
                em(eng, lambda e, dst=dst, src=src, sem=sem: e.dma_start(dst, src).then_inc(S.h[sem], 16))
                return S.c[sem]

            def incQ(eng, sem, fn):
                S.c[sem] += 1
                TRACE[eng].append(("inc", sem, 1))
                em(eng, lambda e, fn=fn, sem=sem: fn(e).then_inc(S.h[sem], 1))
                return S.c[sem]

            def flQ(eng, fn):
                S.c["fl"] += 1
                cnt = S.c["fl"]
                TRACE[eng].append(("inc", "fl", 1))
                TRACE[eng].append(("wait", "fl", cnt))
                em(eng, lambda e, fn=fn, cnt=cnt: (fn(e).then_inc(S.h["fl"], 1),
                                                   e.wait_ge(S.h["fl"], cnt)))
            return Q, em, wv, wr, dmaQ, incQ, flQ

        eye_bf = sb("eye_bf", [128, 128], BF16)
        eye_f = sb("eye_f", [128, 128], F32)
        ones1f = sb("ones1f", [1, 128], F32)
        onescol = sb("onescol", [128, 1], BF16)
        selbf = sb("selbf", [8, 768], BF16)
        wghi = sb("wghi", [128, 2, 16, 8], BF16)
        wglo = sb("wglo", [128, 2, 16, 8], BF16)
        bgb = sb("bgb", [128, 16], F32)
        bcols = sb("bcols", [128, 120], F32)
        bext_bf = sb("bext_bf", [8, D], BF16)
        osc_col = sb("osc_col", [128, 1], F32)

        with contextlib.ExitStack() as pa:
            sa = lambda name, shape, dt: pa.enter_context(nc.sbuf_tensor(name, shape, dt))

            wo_sb = sa("wo_sb", [128, ADT, D], BF16)
            wloc = sa("wloc", [128, 3, 128], BF16)
            cmb = sa("cmb", [128, 3, 128], BF16)
            Isc = sa("Isc", [128, 3, 128], BF16)
            wtmp = sa("wtmp", [128, 128], BF16)
            dvec = sa("dvec", [128, 12], F32)
            dvecs = [dvec[:, i : i + 1] for i in range(12)]
            decu_sb = sa("decu_sb", [128, 8], F32)
            rowd = sa("rowd", [1, 4], F32)
            wg3_sb = sa("wg3_sb", [128, KD, 3], BF16)

            with contextlib.ExitStack() as ws:
                psA_misc = ws.enter_context(nc.psum_tensor("psW_misc", [128, 512], F32))
                scr_f = ws.enter_context(nc.sbuf_tensor("scr_f", [128, 160], F32))
                ubits = ws.enter_context(nc.sbuf_tensor("ubits", [128, 7], F32))
                c1bits = ws.enter_context(nc.sbuf_tensor("c1bits", [128, 7], F32))
                expnbits = ws.enter_context(nc.sbuf_tensor("expnbits", [128, 896], BF16))
                causT = ws.enter_context(nc.sbuf_tensor("causT", [128, 128], BF16))

                with nc.Block() as blk:
                    @blk.vector
                    def _(v):
                        v.memset(bext_bf[:], 0.0)
                with nc.Block() as blk:
                    @blk.sync
                    def _(e):
                        for dst, src in [
                            (eye_bf[:], eye_bf_d), (eye_f[:], eye_f_d), (ones1f[:], ones1f_d),
                            (onescol[:], onescol_d), (selbf[:], selbf_d), (ubits[:], ubits_d),
                            (c1bits[:], c1bits_d), (expnbits[:], expnbits_d), (causT[:], causT_d),
                            (bcols[:], biascols), (rowd[0:1, 0:3], dlog),
                            (scr_f[0:1, 0:1], oscale), (scr_f[0:1, 1:17], bgh),
                        ]:
                            S.dma(e, dst, src[:], "ld")

                    @blk.gpsimd
                    def _(g):
                        S.dma(g, bext_bf[0:7, :], bexth[:], "st")
                        S.dma(g, wqkv_s[0], wq.rearrange("(k p) d -> p k d", p=128), "st")
                        for i in range(3):
                            S.dma(g, wqkv_s[1 + 2 * i], wk[i].rearrange("(k p) d -> p k d", p=128), "st")
                            S.dma(g, wqkv_s[2 + 2 * i], wv[i].rearrange("(k p) d -> p k d", p=128), "st")
                        S.dma(g, wo_sb[:], wo.rearrange("(k p) d -> p k d", p=128), "st")
                        S.dma(g, wg3_sb[:], wg3h.rearrange("p (k d) -> p k d", d=3), "st")
                        for u in range(NU):
                            if u < 6:
                                upsrc = fwup[u].rearrange("(k p) e -> p k e", p=128)
                                swsrc = fwsw[u].rearrange("(k p) e -> p k e", p=128)
                                dnsrc = fwdn[u].rearrange("(k p) (m j) -> p k m j", p=128, j=128)
                            else:
                                upsrc = fwsu.rearrange("(k p) e -> p k e", p=128)[:, :, (u - 6) * E : (u - 5) * E]
                                swsrc = fwsa.rearrange("(k p) e -> p k e", p=128)
                                dnsrc = fwsd[(u - 6) * E : (u - 5) * E, :].rearrange("(k p) (m j) -> p k m j", p=128, j=128)
                            for cc in range(8):
                                S.dma(g, wup_s[u, cc], upsrc[:, :, cc * 128 : (cc + 1) * 128], "st")
                                S.dma(g, wsw_s[u, cc], swsrc[:, :, cc * 128 : (cc + 1) * 128], "st")
                            for kk in range(ET):
                                S.dma(g, wdn_s[:, :, u, kk, :].rearrange("m p j -> p m j"),
                                      dnsrc[:, kk], "st")

                with nc.Block() as blk:
                    @blk.tensor
                    def _(t):
                        t.wait_ge(S.h["ld"], S.c["ld"])
                        t.matmul(psA_misc[:, 0:1], ones1f[:], scr_f[0:1, 0:1], start=True, stop=True)
                        t.matmul(psA_misc[:, 16:32], ones1f[:], scr_f[0:1, 1:17], start=True, stop=True)
                with nc.Block() as blk:
                    @blk.vector
                    def _(v):
                        v.tensor_copy(osc_col[:], psA_misc[:, 0:1])
                        v.tensor_copy(bgb[:], psA_misc[:, 16:32])
                with nc.Block() as blk:
                    @blk.vector
                    def _(v):
                        v.wait_ge(S.h["st"], S.c["st"])

                for j in range(2):
                    with nc.Block() as blk:
                        @blk.sync
                        def _(e, j=j):
                            S.dma(e, scr_f[:, 32:160], fgh[j], "ld")
                    with nc.Block() as blk:
                        @blk.vector
                        def _(v, j=j):
                            v.wait_ge(S.h["ld"], S.c["ld"])
                            hi = wghi[:, j].rearrange("p k d -> p (k d)")
                            lo = wglo[:, j].rearrange("p k d -> p (k d)")
                            S.flush(v, v.tensor_copy(hi, scr_f[:, 32:160]))
                            v.tensor_tensor(lo, scr_f[:, 32:160], hi, ALU.subtract)

                for i in range(3):
                    with nc.Block() as blk:
                        @blk.tensor
                        def _(t, i=i):
                            t.matmul(psA_misc[:, 32:33], ones1f[:], rowd[0:1, i : i + 1], start=True, stop=True)
                    with nc.Block() as blk:
                        @blk.scalar
                        def _(sc):
                            sc.activation(dvecs[0], psA_misc[:, 32:33], AF.Sigmoid)
                    with nc.Block() as blk:
                        @blk.vector
                        def _(v, i=i):
                            F = lambda inst: S.flush(v, inst)
                            for b in range(1, 7):
                                F(v.tensor_mul(dvecs[b], dvecs[b - 1], dvecs[b - 1]))
                            F(v.tensor_mul(dvecs[7], dvecs[6], dvecs[6]))
                            for tgt, bits in ((i, ubits), (3 + i, c1bits)):
                                cur = 8
                                for b in range(7):
                                    F(v.tensor_scalar_add(dvecs[10], dvecs[b], -1.0))
                                    F(v.tensor_scalar(dvecs[11], bits[:, b : b + 1], dvecs[10], 1.0, ALU.mult, ALU.add))
                                    if b == 0:
                                        F(v.tensor_copy(dvecs[cur], dvecs[11]))
                                    else:
                                        F(v.tensor_mul(dvecs[cur ^ 1], dvecs[cur], dvecs[11]))
                                        cur ^= 1
                                F(v.tensor_copy(decu_sb[:, tgt : tgt + 1], dvecs[cur]))
                            F(v.tensor_copy(wloc[:, i], causT[:]))
                            cur = 0
                            for b in range(7):
                                F(v.tensor_scalar_add(dvecs[10], dvecs[b], -1.0))
                                F(v.tensor_scalar(wtmp[:], expnbits[:, b * 128 : (b + 1) * 128], dvecs[10], 1.0, ALU.mult, ALU.add))
                                if cur == 0:
                                    F(v.tensor_mul(cmb[:, i], wloc[:, i], wtmp[:]))
                                else:
                                    F(v.tensor_mul(wloc[:, i], cmb[:, i], wtmp[:]))
                                cur ^= 1
                            if cur == 1:
                                F(v.tensor_copy(wloc[:, i], cmb[:, i]))
                            F(v.tensor_scalar(Isc[:, i], eye_bf[:], dvecs[7], None, ALU.mult))

            RQ, RKF, RKT, RV, RGB, RW = 0, 512, 2048, 3584, 5120, 5888

            psR = pa.enter_context(nc.psum_tensor("psR", [128, 2048], F32))
            psBC = pa.enter_context(nc.psum_tensor("psBC", [128, 1024], F32))
            psTP = pa.enter_context(nc.psum_tensor("psTP", [128, 1024], BF16))
            psX = pa.enter_context(nc.psum_tensor("psX", [128, 512], F32))

            fb = sa("fb", [128, KD, TB], F32)
            st1 = sa("st1", [128, 2, TB], BF16)
            st2 = sa("st2", [128, 2, TB], BF16)
            xn = sa("xn", [128, KD, TB], BF16)
            recb = sa("recb", [128, 4, RW], BF16)
            wpj = sa("wpj", [128, 2, KD, AD], BF16)
            qkt = sa("qkt", [128, 2, TB], BF16)
            gwork = sa("gwork", [128, 16], F32)
            gdT_sb = sa("gdT_sb", [8, 128], BF16)
            S_sb = sa("S_sb", [128, 3, ADT, AD], BF16)
            qt = sa("qt", [128, 3, AD], BF16)
            cmbA = sa("cmbA", [128, 3, 128], BF16)
            scb = sa("scb", [128, 3, 128], BF16)
            retrbf = sa("retrbf", [128, AD], BF16)
            xs = sa("xs", [128, 2, TB], F32)
            ta0 = sa("ta0", [128, TB], F32)
            hl = sa("hl", [128, 2, 2, TB], BF16)
            rowsA = sa("rowsA", [1, 3584], F32)
            RS = [rowsA[0:1, i * 512:(i + 1) * 512] for i in range(7)]
            recb_h = recb.tensor if hasattr(recb, "tensor") else recb

            QA, emA, wvA, wrA, dmaA, incA, flA = mk_rec()

            def rows_mathA(s1ap, s2ap, mmcnt):
                wvA("vector", "mm", mmcnt)
                flA("vector", lambda e: e.tensor_scalar(RS[0], s1ap, 1.0 / D, None, ALU.mult))
                flA("vector", lambda e: e.tensor_scalar(RS[1], s2ap, 1.0 / D, None, ALU.mult))
                flA("vector", lambda e: e.tensor_mul(RS[2], RS[0], RS[0]))
                flA("vector", lambda e: e.tensor_sub(RS[4], RS[1], RS[2]))
                ve = incA("vector", "evv", lambda e: e.tensor_scalar_add(RS[3], RS[4], EPS))
                wvA("scalar", "evv", ve)
                flA("scalar", lambda e: e.activation(RS[2], RS[3], AF.Ln))
                se = incA("scalar", "ev", lambda e: e.activation(RS[4], RS[2], AF.Exp, bias=0.0, scale=-0.5))
                wvA("vector", "ev", se)
                flA("vector", lambda e: e.tensor_mul(RS[1], RS[4], RS[4]))
                flA("vector", lambda e: e.tensor_mul(RS[2], RS[1], RS[3]))
                flA("vector", lambda e: e.tensor_scalar(RS[1], RS[2], -0.5, 1.5, ALU.mult, ALU.add))
                flA("vector", lambda e: e.tensor_mul(RS[6], RS[4], RS[1]))
                return incA("vector", "evv", lambda e: e.tensor_mul(RS[5], RS[0], RS[6]))

            bankA = [None, None, None, None]
            bA_i = [0]

            def acqA():
                b = bA_i[0] % 4
                bA_i[0] += 1
                wrA("tensor", bankA[b])
                return b

            def bcs(e):
                e.matmul(psBC[:, 0:512], ones1f[:], RS[5], start=True, stop=True)
                return e.matmul(psBC[:, 512:1024], ones1f[:], RS[6], start=True, stop=True)

            wvA("sync", "st", S.c["st"])
            memev = incA("vector", "evv", lambda e: e.memset(S_sb[:].rearrange("p a b c -> p (a b c)"), 0.0))
            supd_done = memev
            wp_i = [0]
            wpj_ld = {}
            wpj_cons = {}
            PJS = [("q", None), ("k", 0), ("v", 0), ("k", 1), ("v", 1), ("k", 2), ("v", 2)]

            def issue_wpj():
                idx = wp_i[0]
                if idx >= 7 * NB:
                    return
                r = wpj_cons.get(idx - 2)
                if r is not None:
                    wvA("sync", "mm", r)
                wpj_ld[idx] = dmaA("sync", "wld", wpj[:, idx % 2], wqkv_s[idx % 7])
                wp_i[0] += 1

            issue_wpj()
            issue_wpj()

            fb_free = None
            fb_free_st = None
            bc_free = None
            psX_rel = None
            scan_done_mm = None
            prev_retr_mm = None
            sel_last_mm = None
            gate_tmm_prev = None
            hl_carry = {0: None, 1: None}
            tp_state = {"rel": None, "last_tmm": None}
            pend_tp = []

            def emit_tp():
                kind, i, slot, ev = pend_tp.pop(0)
                wvA("tensor", "evv", ev)
                if tp_state["rel"] is not None:
                    wrA("tensor", tp_state["rel"])

                def tmms(e, slot=slot):
                    for jj in range(ADT):
                        mi = e.transpose(psTP[:, jj * 128:(jj + 1) * 128],
                                         qkt[:, slot, jj * 128:(jj + 1) * 128], eye_bf[:])
                    return mi
                tmm = incA("tensor", "mm", tmms)
                tp_state["last_tmm"] = tmm
                wvA("vector", "mm", tmm)
                sub, off = kind
                tev = incA("vector", "evv",
                           lambda e, sub=sub, off=off: e.tensor_copy(recb[:, sub, off:off + 512], psTP[:, 0:512]))
                tp_state["rel"] = ("evv", tev)

            for gi in range(NB if ABLOCKS is None else ABLOCKS):
                g = NB - 1 - gi
                wrA("sync", fb_free)
                wrA("sync", fb_free_st)
                fb_ld = dmaA("sync", "ld", fb[:], fm(xT)[:, :, g * TB:(g + 1) * TB])

                stat_mm = {}
                for k in range(KD):
                    if k == 0:
                        wvA("scalar", "ld", fb_ld)
                    if k >= 2:
                        wvA("scalar", "mm", stat_mm[k - 2])
                    incA("scalar", "ev", lambda e, k=k: e.activation(st1[:, k % 2], fb[:, k], AF.Identity))
                    e2 = incA("scalar", "ev", lambda e, k=k: e.activation(st2[:, k % 2], fb[:, k], AF.Square))
                    wvA("tensor", "ev", e2)
                    if k == 0:
                        wrA("tensor", bc_free)

                    def smms(e, k=k):
                        e.matmul(psBC[0:1, 0:512], onescol[:], st1[:, k % 2], start=(k == 0), stop=(k == KD - 1))
                        return e.matmul(psBC[0:1, 512:1024], onescol[:], st2[:, k % 2], start=(k == 0), stop=(k == KD - 1))
                    stat_mm[k] = incA("tensor", "mm", smms)
                ln1_scalar_ev = S.c["ev"]
                rev = rows_mathA(psBC[0:1, 0:512], psBC[0:1, 512:1024], stat_mm[KD - 1])
                wvA("tensor", "evv", rev)
                bc_mm = incA("tensor", "mm", bcs)

                if dbg and gi == 0:
                    wvA("gpsimd", "mm", bc_mm)
                    dmaA("gpsimd", "st", dbgout[0:1, 8192:8192 + 3584], rowsA[:])
                aev = None
                for k in range(KD):
                    if k == 0:
                        wvA("vector", "mm", bc_mm)
                        wvA("vector", "ld", fb_ld)
                    flA("vector", lambda e, k=k: e.tensor_tensor(ta0[:], fb[:, k], psBC[:, 512:1024], ALU.mult))
                    aev = incA("vector", "evv", lambda e, k=k: e.tensor_tensor(xn[:, k, :], ta0[:], psBC[:, 0:512], ALU.subtract))
                apply_done = aev
                bc_free = ("evv", apply_done)

                qk_ctr = [0]
                for pj in range(7):
                    kind, i = PJS[pj]
                    widx = gi * 7 + pj
                    wslot = widx % 2
                    wvA("tensor", "wld", wpj_ld[widx])
                    for sub in range(4):
                        b = acqA()
                        if pj == 0 and sub == 0:
                            wvA("tensor", "evv", apply_done)

                        def pmms(e, sub=sub, b=b, wslot=wslot):
                            for kk in range(KD):
                                mi = e.matmul(psR[:, b * 512:(b + 1) * 512], xn[:, kk, sub * 128:(sub + 1) * 128],
                                              wpj[:, wslot, kk, :], start=(kk == 0), stop=(kk == KD - 1))
                            return mi
                        pmm = incA("tensor", "mm", pmms)
                        if pend_tp:
                            emit_tp()
                        wvA("vector", "mm", pmm)
                        if pj == 0 and sub == 0 and scan_done_mm is not None:
                            wvA("vector", "mm", scan_done_mm)
                        sl = psR[:, b * 512:(b + 1) * 512]
                        if kind == "v":
                            ev = incA("vector", "evv",
                                      lambda e, sl=sl, sub=sub, i=i: e.tensor_copy(
                                          recb[:, sub, RV + i * 512:RV + (i + 1) * 512], sl))
                        else:
                            tslot = qk_ctr[0] % 2
                            qk_ctr[0] += 1
                            if tp_state["last_tmm"] is not None:
                                wvA("vector", "mm", tp_state["last_tmm"])
                            if kind == "q":
                                ev = incA("vector", "evv",
                                          lambda e, sl=sl, tslot=tslot: e.tensor_copy(qkt[:, tslot], sl))
                                pend_tp.append(((sub, RQ), i, tslot, ev))
                            else:
                                flA("vector", lambda e, sl=sl, tslot=tslot: e.tensor_copy(qkt[:, tslot], sl))
                                ev = incA("vector", "evv",
                                          lambda e, sl=sl, sub=sub, i=i: e.tensor_scalar(
                                              recb[:, sub, RKT + i * 512:RKT + (i + 1) * 512], sl,
                                              decu_sb[:, i:i + 1], None, ALU.mult))
                                pend_tp.append(((sub, RKF + i * 512), i, tslot, ev))
                        bankA[b] = ("evv", ev)
                    wpj_cons[widx] = S.c["mm"]
                    issue_wpj()
                while pend_tp:
                    emit_tp()
                if dbg and gi == 0:
                    wvA("gpsimd", "evv", apply_done)
                    dmaA("gpsimd", "st", dbgb[:, 0:8192].rearrange("p (k t) -> p k t", t=TB), xn[:])

                rec_done = {}
                for sub in range(4):
                    wrA("tensor", psX_rel)

                    def gmms(e, sub=sub):
                        for kk in range(KD):
                            mi = e.matmul(psX[:, 256:259], xn[:, kk, sub * 128:(sub + 1) * 128],
                                          wg3_sb[:, kk, :], start=(kk == 0), stop=(kk == KD - 1))
                        return mi
                    gmm = incA("tensor", "mm", gmms)
                    wvA("vector", "mm", gmm)
                    ge = incA("vector", "evv", lambda e: e.tensor_copy(gwork[:, 0:3], psX[:, 256:259]))
                    psX_rel = ("evv", ge)
                    wvA("vector", "evv", ge)
                    wvA("scalar", "evv", ge)
                    se = incA("scalar", "ev", lambda e: e.activation(gwork[:, 3:6], gwork[:, 0:3], AF.Exp))
                    wvA("vector", "ev", se)
                    flA("vector", lambda e: e.reduce_sum(gwork[:, 6:7], gwork[:, 3:6], axis=AX.X))
                    flA("vector", lambda e: e.reciprocal(gwork[:, 7:8], gwork[:, 6:7]))
                    if gate_tmm_prev is not None:
                        wvA("vector", "mm", gate_tmm_prev)
                    flA("vector", lambda e: e.tensor_scalar(gwork[:, 8:11], gwork[:, 3:6], gwork[:, 7:8], None, ALU.mult))
                    for i3 in range(2):
                        emA("vector", lambda e, i3=i3: e.tensor_scalar(gwork[:, 11 + i3:12 + i3],
                                                                       gwork[:, 8 + i3:9 + i3],
                                                                       decu_sb[:, 3 + i3:4 + i3], None, ALU.mult))
                    gdev = incA("vector", "evv",
                                lambda e: e.tensor_scalar(gwork[:, 13:14], gwork[:, 10:11],
                                                          decu_sb[:, 5:6], None, ALU.mult))
                    wvA("tensor", "evv", gdev)
                    wrA("tensor", psX_rel)
                    tmm = incA("tensor", "mm", lambda e: e.transpose(psX[0:6, 128:256], gwork[:, 8:14], eye_f[:]))
                    gate_tmm_prev = tmm
                    wvA("vector", "mm", tmm)
                    if sel_last_mm is not None:
                        wvA("vector", "mm", sel_last_mm)
                    gt = incA("vector", "evv", lambda e: e.tensor_copy(gdT_sb[0:6, :], psX[0:6, 128:256]))
                    psX_rel = ("evv", gt)
                    wvA("tensor", "evv", gt)
                    for n in range(6):
                        b = acqA()
                        smm = incA("tensor", "mm",
                                   lambda e, n=n, b=b: e.matmul(psR[:, b * 512:b * 512 + 128],
                                                                selbf[0:6, n * 128:(n + 1) * 128], gdT_sb[0:6, :],
                                                                start=True, stop=True))
                        wvA("vector", "mm", smm)
                        i3 = n % 3
                        off = RGB + (i3 * 128 if n < 3 else 384 + i3 * 128)
                        ev = incA("vector", "evv",
                                  lambda e, sub=sub, off=off, b=b: e.tensor_copy(recb[:, sub, off:off + 128],
                                                                                 psR[:, b * 512:b * 512 + 128]))
                        bankA[b] = ("evv", ev)
                        sel_last_mm = smm
                    rec_done[sub] = S.c["evv"]

                wo_evacs_started = False
                for sub in (3, 2, 1, 0):
                    roff = sub * RW
                    prep_ev = None
                    if prev_retr_mm is not None:
                        wvA("vector", "mm", prev_retr_mm)
                    for i in range(3):
                        flA("vector", lambda e, sub=sub, i=i: e.tensor_tensor(
                            cmbA[:, i], wloc[:, i], recb[:, sub, RGB + i * 128:RGB + (i + 1) * 128], ALU.mult))
                        brep = bass.AP(recb_h, roff + RGB + 384 + i * 128, [[4 * RW, 128], [0, ADT], [1, 128]])
                        prep_ev = incA("vector", "evv", lambda e, sub=sub, i=i, brep=brep: e.tensor_tensor(
                            qt[:, i], recb[:, sub, RQ:RQ + 512].rearrange("p (a b) -> p a b", b=128), brep, ALU.mult))

                    rb = acqA()
                    first_retr = True
                    retr_mm = None
                    for i in range(3):
                        wrA("tensor", psX_rel)
                        if i == 0:
                            wvA("tensor", "evv", max(rec_done[sub], supd_done, prep_ev))

                        def scmms(e, sub=sub, i=i):
                            for kd in range(ADT):
                                mi = e.matmul(psX[:, 0:128],
                                              recb[:, sub, RKF + i * 512 + kd * 128:RKF + i * 512 + (kd + 1) * 128],
                                              recb[:, sub, RQ + kd * 128:RQ + (kd + 1) * 128],
                                              start=(kd == 0), stop=(kd == ADT - 1))
                            return mi
                        smm = incA("tensor", "mm", scmms)
                        wvA("vector", "mm", smm)
                        scev = incA("vector", "evv",
                                    lambda e, i=i: e.tensor_tensor(scb[:, i], psX[:, 0:128], cmbA[:, i], ALU.mult))
                        psX_rel = ("evv", scev)
                        wvA("tensor", "evv", scev)

                        def rmms(e, sub=sub, i=i, rb=rb, fr=first_retr):
                            for m in range(ADT):
                                e.matmul(psR[:, rb * 512 + m * 128:rb * 512 + (m + 1) * 128],
                                         recb[:, sub, RV + i * 512 + m * 128:RV + i * 512 + (m + 1) * 128],
                                         scb[:, i], start=fr, stop=False)
                                fr = False
                            for m in range(ADT):
                                for kd in range(ADT):
                                    last = (i == 2 and m == ADT - 1 and kd == ADT - 1)
                                    mi = e.matmul(psR[:, rb * 512 + m * 128:rb * 512 + (m + 1) * 128],
                                                  S_sb[:, i, kd, m * 128:(m + 1) * 128],
                                                  qt[:, i, kd * 128:(kd + 1) * 128], start=False, stop=last)
                            return mi
                        retr_mm = incA("tensor", "mm", rmms)
                        first_retr = False
                    prev_retr_mm = retr_mm
                    wvA("vector", "mm", retr_mm)
                    rev2 = incA("vector", "evv",
                                lambda e, rb=rb: e.tensor_scalar(retrbf[:], psR[:, rb * 512:(rb + 1) * 512],
                                                                 osc_col[:], None, ALU.mult))
                    bankA[rb] = ("evv", rev2)

                    for i in range(3):
                        for m in range(ADT):
                            b2 = acqA()

                            def umms(e, sub=sub, i=i, m=m, b2=b2):
                                e.matmul(psR[:, b2 * 512:(b2 + 1) * 512],
                                         recb[:, sub, RKT + i * 512 + m * 128:RKT + i * 512 + (m + 1) * 128],
                                         recb[:, sub, RV + i * 512:RV + (i + 1) * 512], start=True, stop=False)
                                return e.matmul(psR[:, b2 * 512:(b2 + 1) * 512], Isc[:, i],
                                                S_sb[:, i, m, :], start=False, stop=True)
                            umm = incA("tensor", "mm", umms)
                            wvA("vector", "mm", umm)
                            uev = incA("vector", "evv",
                                       lambda e, i=i, m=m, b2=b2: e.tensor_copy(S_sb[:, i, m, :],
                                                                                psR[:, b2 * 512:(b2 + 1) * 512]))
                            bankA[b2] = ("evv", uev)
                    supd_done = S.c["evv"]

                    wvA("tensor", "evv", rev2)
                    for mg in range(4):
                        b3 = acqA()

                        def wmms(e, mg=mg, b3=b3):
                            for lm in range(4):
                                m = mg * 4 + lm
                                for kd in range(ADT):
                                    mi = e.matmul(psR[:, b3 * 512 + lm * 128:b3 * 512 + (lm + 1) * 128],
                                                  wo_sb[:, kd, m * 128:(m + 1) * 128],
                                                  retrbf[:, kd * 128:(kd + 1) * 128],
                                                  start=(kd == 0), stop=(kd == ADT - 1))
                            return mi
                        wmm = incA("tensor", "mm", wmms)
                        wvA("vector", "mm", wmm)
                        if not wo_evacs_started:
                            wvA("vector", "ev", ln1_scalar_ev)
                            wo_evacs_started = True
                        wev = incA("vector", "evv",
                                   lambda e, sub=sub, mg=mg, b3=b3: e.tensor_copy(
                                       fb[:, mg * 4:(mg + 1) * 4, sub * 128:(sub + 1) * 128],
                                       psR[:, b3 * 512:(b3 + 1) * 512].rearrange("p (a b) -> p a b", b=128)))
                        bankA[b3] = ("evv", wev)
                scan_done_mm = S.c["mm"]
                ao_done = S.c["evv"]
                if dbg and gi == 0:
                    wvA("gpsimd", "evv", ao_done)
                    dmaA("gpsimd", "st", dbgout[:, 0:8192].rearrange("p (k t) -> p k t", t=TB), fb[:])

                stat2_mm = {}
                for k in range(KD):
                    if k == 0:
                        wvA("scalar", "evv", ao_done)
                    if k >= 2:
                        wvA("scalar", "mm", stat2_mm[k - 2])
                    incA("scalar", "ev", lambda e, k=k: e.activation(st1[:, k % 2], fb[:, k], AF.Identity))
                    e2 = incA("scalar", "ev", lambda e, k=k: e.activation(st2[:, k % 2], fb[:, k], AF.Square))
                    wvA("tensor", "ev", e2)
                    if k == 0:
                        wrA("tensor", bc_free)

                    def smms2(e, k=k):
                        e.matmul(psBC[0:1, 0:512], onescol[:], st1[:, k % 2], start=(k == 0), stop=(k == KD - 1))
                        return e.matmul(psBC[0:1, 512:1024], onescol[:], st2[:, k % 2], start=(k == 0), stop=(k == KD - 1))
                    stat2_mm[k] = incA("tensor", "mm", smms2)
                rev = rows_mathA(psBC[0:1, 0:512], psBC[0:1, 512:1024], stat2_mm[KD - 1])
                wvA("tensor", "evv", rev)
                bc2_mm = incA("tensor", "mm", bcs)

                sb1 = acqA()
                bankA[sb1] = ("evv", 10 ** 9)
                sb2 = acqA()
                bankA[sb2] = ("evv", 10 ** 9)
                yst = {}
                yev_l = {}
                stat3_mm = {}
                for k in range(KD):
                    if k >= 2:
                        wvA("scalar", "evv", yev_l[k - 2])
                    xld = dmaA("scalar", "xld", xs[:, k % 2], fm(xT)[:, k, g * TB:(g + 1) * TB])
                    wvA("vector", "xld", xld)
                    if k == 0:
                        wvA("vector", "mm", bc2_mm)
                    flA("vector", lambda e, k=k: e.tensor_tensor(ta0[:], fb[:, k], psBC[:, 512:1024], ALU.mult))
                    flA("vector", lambda e: e.tensor_tensor(ta0[:], ta0[:], psBC[:, 0:512], ALU.subtract))
                    yev = incA("vector", "evv", lambda e, k=k: e.tensor_tensor(fb[:, k], ta0[:], xs[:, k % 2], ALU.add))
                    yev_l[k] = yev
                    wvA("gpsimd", "evv", yev)
                    yst[k] = dmaA("gpsimd", "st", fm(y_s)[:, k, g * TB:(g + 1) * TB], fb[:, k, :])
                    wvA("scalar", "evv", yev)
                    if k >= 2:
                        wvA("scalar", "mm", stat3_mm[k - 2])
                    incA("scalar", "ev", lambda e, k=k: e.activation(st1[:, k % 2], fb[:, k], AF.Identity))
                    e2 = incA("scalar", "ev", lambda e, k=k: e.activation(st2[:, k % 2], fb[:, k], AF.Square))
                    wvA("tensor", "ev", e2)

                    def smms3(e, k=k, sb1=sb1, sb2=sb2):
                        e.matmul(psR[0:1, sb1 * 512:sb1 * 512 + 512], onescol[:], st1[:, k % 2],
                                 start=(k == 0), stop=(k == KD - 1))
                        return e.matmul(psR[0:1, sb2 * 512:sb2 * 512 + 512], onescol[:], st2[:, k % 2],
                                        start=(k == 0), stop=(k == KD - 1))
                    stat3_mm[k] = incA("tensor", "mm", smms3)
                rev = rows_mathA(psR[0:1, sb1 * 512:sb1 * 512 + 512], psR[0:1, sb2 * 512:sb2 * 512 + 512], stat3_mm[KD - 1])
                bankA[sb1] = ("evv", rev)
                bankA[sb2] = ("evv", rev)
                wvA("tensor", "evv", rev)
                bc3_mm = incA("tensor", "mm", bcs)

                hlst = {}
                hlev = None
                for k in range(KD):
                    if k == 0:
                        wvA("vector", "mm", bc3_mm)
                    if k >= 2:
                        wvA("vector", "st", hlst[k - 2])
                    elif hl_carry[k] is not None:
                        wrA("vector", hl_carry[k])
                    flA("vector", lambda e, k=k: e.tensor_tensor(ta0[:], fb[:, k], psBC[:, 512:1024], ALU.mult))
                    flA("vector", lambda e: e.tensor_tensor(ta0[:], ta0[:], psBC[:, 0:512], ALU.subtract))
                    flA("vector", lambda e, k=k: e.tensor_copy(hl[:, k % 2, 0], ta0[:]))
                    hlev = incA("vector", "evv",
                                lambda e, k=k: e.tensor_tensor(hl[:, k % 2, 1], ta0[:], hl[:, k % 2, 0], ALU.subtract))
                    wvA("gpsimd", "evv", hlev)
                    dmaA("gpsimd", "st", fm(r0hi_s)[:, k, g * TB:(g + 1) * TB], hl[:, k % 2, 0])
                    hlst[k] = dmaA("gpsimd", "st", fm(r0lo_s)[:, k, g * TB:(g + 1) * TB], hl[:, k % 2, 1])
                fb_free = ("evv", hlev)
                fb_free_st = ("st", yst[KD - 1])
                bc_free = ("evv", hlev)
                hl_carry = {0: ("st", hlst[KD - 2]), 1: ("st", hlst[KD - 1])}

            with nc.Block() as blk:
                @blk.tensor
                def _(te):
                    for fn in QA["tensor"]:
                        fn(te)

                @blk.scalar
                def _(sc):
                    for fn in QA["scalar"]:
                        fn(sc)

                @blk.vector
                def _(v):
                    for fn in QA["vector"]:
                        fn(v)

                @blk.sync
                def _(e):
                    for fn in QA["sync"]:
                        fn(e)

                @blk.gpsimd
                def _(gp):
                    for fn in QA["gpsimd"]:
                        fn(gp)


        TT = 1024
        NT = T // TT
        with contextlib.ExitStack() as pb:
            sbb = lambda name, shape, dt: pb.enter_context(nc.sbuf_tensor(name, shape, dt))
            psA = pb.enter_context(nc.psum_tensor("psA", [128, 2048], F32))
            psW = pb.enter_context(nc.psum_tensor("psW", [128, 1024], F32))
            psG = pb.enter_context(nc.psum_tensor("psG", [128, 512], F32))
            psS = pb.enter_context(nc.psum_tensor("psS", [128, 512], F32))

            xtu = sbb("xtu", [128, KD, TT], BF16)
            ph = sbb("ph", [128, KD, TT], BF16)
            acc = sbb("acc", [128, HT, TT], F32)
            upb = sbb("upb", [128, 3, KD, 128], BF16)
            swb = sbb("swb", [128, 4, ET, 128], BF16)
            dnb = sbb("dnb", [128, 4, ET, 128], BF16)
            wbn = sbb("wbn", [128, TT], BF16)
            wTx = sbb("wTx", [8, TT], BF16)
            xgl = sbb("xgl", [128, 2, KD, 128], BF16)
            rrow = sbb("rrow", [128, 2, TT], F32)
            rb16 = sbb("rb16", [128, 2, 2, TT], BF16)
            stg = sbb("stg", [128, 2, TT], F32)
            zsqb = sbb("zsqb", [128, TT], BF16)
            gg = sbb("gg", [128, 56], F32)

            Q, em, wv, wr, dmaQ, incQ, flQ = mk_rec()

            TOT_TILES = 2 * NT
            TOT_UP = TOT_TILES * NU * 8
            TOT_SW = TOT_TILES * NU * 8
            TOT_DN = TOT_TILES * NU * HT
            up_ld = {}; up_cons = {}
            sw_ld = {}; sw_cons = {}
            dn_ld = {}; dn_cons = {}

            def issue_up(g):
                if g >= TOT_UP:
                    return
                n, c = divmod(g % (NU * 8), 8)
                r = up_cons.get(g - 3)
                if r is not None:
                    wv("sync", "mm", r)
                up_ld[g] = dmaQ("sync", "wld", upb[:, g % 3], wup_s[n, c])

            def issue_sw(g):
                if g >= TOT_SW:
                    return
                n, c = divmod(g % (NU * 8), 8)
                r = sw_cons.get(g - 4)
                if r is not None:
                    wv("sync", "mm", r)
                sw_ld[g] = dmaQ("sync", "wld", swb[:, g % 4], wsw_s[n, c])

            def issue_dn(g):
                if g >= TOT_DN:
                    return
                n, m = divmod(g % (NU * HT), HT)
                r = dn_cons.get(g - 4)
                if r is not None:
                    wv("gpsimd", "mm", r)
                dn_ld[g] = dmaQ("gpsimd", "dld", dnb[:, g % 4], wdn_s[m, :, n, :, :])

            bankrel = [None, None, None, None]
            bank_i = [0]

            def acquire_bank():
                b = bank_i[0] % 4
                bank_i[0] += 1
                wr("tensor", bankrel[b])
                return b

            carry = {"xtu": None, "ph": None, "psG": None, "psW": None, "wTx": None,
                     "stg": None, "rrow": {0: None, 1: None}, "rb16": {0: None, 1: None},
                     "xgl": {0: None, 1: None}}
            zstore = {}
            tile_iter = 0
            for j in range(2):
                hi_src = r0hi_s if j == 0 else xt1hi_s
                lo_src = r0lo_s if j == 0 else xt1lo_s
                for t in range(NT):
                    base = t * TT
                    if tile_iter == 0:
                        st_all = S.c["st"]
                        wv("sync", "st", st_all)
                        wv("scalar", "st", st_all)
                        wv("gpsimd", "st", st_all)
                    if j == 1:
                        wv("sync", "st", zstore[t])
                        wv("scalar", "st", zstore[t])
                    wr("sync", carry["xtu"])
                    xtu_ldv = dmaQ("sync", "ld", xtu[:], fm(hi_src)[:, :, base:base + TT])
                    if tile_iter == 0:
                        for g0 in range(3):
                            issue_up(g0)
                        for g0 in range(4):
                            issue_sw(g0)
                        for g0 in range(4):
                            issue_dn(g0)

                    wr("vector", carry["wTx"])
                    flQ("vector", lambda e: e.memset(wTx[:], 1.0))
                    G8 = gg[:, 0:8]
                    g_mm = {}; gate_rel = {}; gtr = {}; wTxrel = {}
                    for c in range(8):
                        if c >= 2:
                            wv("scalar", "mm", g_mm[c - 2])
                        else:
                            wr("scalar", carry["xgl"][c])
                        xldc = dmaQ("scalar", "xld", xgl[:, c % 2],
                                    fm(lo_src)[:, :, base + c * 128:base + (c + 1) * 128])
                        wv("tensor", "xld", xldc)
                        if c == 0:
                            wv("tensor", "ld", xtu_ldv)
                            wr("tensor", carry["psG"])
                        else:
                            wv("tensor", "evv", gate_rel[c - 1])

                        def g_mms(e, c=c, j=j):
                            for k in range(KD):
                                e.matmul(psG[:, 0:8], xtu[:, k, c * 128:(c + 1) * 128],
                                         wghi[:, j, k, :], start=(k == 0), stop=False)
                            for k in range(KD):
                                e.matmul(psG[:, 0:8], xtu[:, k, c * 128:(c + 1) * 128],
                                         wglo[:, j, k, :], start=False, stop=False)
                            for k in range(KD):
                                mi = e.matmul(psG[:, 0:8], xgl[:, c % 2, k, :],
                                              wghi[:, j, k, :], start=False, stop=(k == KD - 1))
                            return mi
                        g_mm[c] = incQ("tensor", "mm", g_mms)

                        wv("vector", "mm", g_mm[c])
                        if c > 0:
                            wv("vector", "mm", gtr[c - 1])
                        gate_rel[c] = incQ("vector", "evv",
                                           lambda e, j=j: e.tensor_tensor(G8, psG[:, 0:8], bgb[:, j * 8:(j + 1) * 8], ALU.add))
                        wv("vector", "evv", gate_rel[c])
                        flQ("vector", lambda e: e.tensor_single_scalar(gg[:, 8:16], G8, G8[:, 0:1], ALU.is_lt))
                        cur = 8
                        for mI in range(1, 8):
                            flQ("vector", lambda e, mI=mI: e.tensor_single_scalar(gg[:, 24:32], G8, G8[:, mI:mI + 1], ALU.is_lt))
                            nxt = 24 - cur
                            flQ("vector", lambda e, cur=cur, nxt=nxt: e.tensor_add(gg[:, nxt:nxt + 8], gg[:, cur:cur + 8], gg[:, 24:32]))
                            cur = nxt
                        flQ("vector", lambda e, cur=cur: e.tensor_single_scalar(gg[:, 32:40], gg[:, cur:cur + 8], 5.5, ALU.is_lt))
                        ggv = incQ("vector", "evv", lambda e: e.reduce_max(gg[:, 48:49], G8, axis=AX.X, negate=True))
                        wv("scalar", "evv", ggv)
                        gexp = incQ("scalar", "ev", lambda e: e.activation(gg[:, 40:48], gg[:, 0:8], AF.Exp, bias=gg[:, 48:49], scale=1.0))
                        wv("vector", "ev", gexp)
                        flQ("vector", lambda e: e.tensor_mul(gg[:, 24:32], gg[:, 40:48], gg[:, 32:40]))
                        flQ("vector", lambda e: e.reduce_sum(gg[:, 49:50], gg[:, 24:32], axis=AX.X))
                        flQ("vector", lambda e: e.reciprocal(gg[:, 50:51], gg[:, 49:50]))
                        gn = incQ("vector", "evv", lambda e: e.tensor_scalar(gg[:, 0:8], gg[:, 24:32], gg[:, 50:51], None, ALU.mult))
                        wv("tensor", "evv", gn)
                        if c > 0:
                            wv("tensor", "evv", wTxrel[c - 1])
                        gtr[c] = incQ("tensor", "mm", lambda e: e.transpose(psG[0:8, 128:256], gg[:, 0:8], eye_f[:]))
                        wv("vector", "mm", gtr[c])
                        wTxrel[c] = incQ("vector", "evv",
                                         lambda e, c=c: e.tensor_copy(wTx[0:6, c * 128:(c + 1) * 128], psG[0:6, 128:256]))
                    wtx_done = wTxrel[7]
                    carry["xgl"] = {0: ("mm", g_mm[6]), 1: ("mm", g_mm[7])}

                    selmm = {}; wbnrel = {}
                    updone = {}; swdone = {}; dndone = {}
                    for n in range(NU):
                        if n < 6:
                            wv("tensor", "evv", wtx_done if n == 0 else wbnrel[n - 1])
                            if n == 0:
                                wr("tensor", carry["psW"])

                            def sel_mms(e, n=n):
                                e.matmul(psW[:, 0:512], selbf[0:8, n * 128:(n + 1) * 128], wTx[0:8, 0:512], start=True, stop=True)
                                return e.matmul(psW[:, 512:1024], selbf[0:8, n * 128:(n + 1) * 128], wTx[0:8, 512:1024], start=True, stop=True)
                            selmm[n] = incQ("tensor", "mm", sel_mms)
                            wv("vector", "mm", selmm[n])
                            wbnrel[n] = incQ("vector", "evv", lambda e: e.tensor_copy(wbn[:], psW[:]))

                        last_up_ev = {}
                        for c in range(8):
                            gidx = tile_iter * 64 + n * 8 + c
                            wv("tensor", "wld", up_ld[gidx])
                            for sub in range(2):
                                b = acquire_bank()
                                if tile_iter == 0 and n == 0 and c == 0 and sub == 0:
                                    wv("tensor", "ld", xtu_ldv)

                                def up_mms(e, c=c, sub=sub, b=b, slot=gidx % 3):
                                    for k in range(KD):
                                        mi = e.matmul(psA[:, b * 512:(b + 1) * 512], upb[:, slot, k, :],
                                                      xtu[:, k, sub * 512:(sub + 1) * 512],
                                                      start=(k == 0), stop=(k == KD - 1))
                                    return mi
                                gm = incQ("tensor", "mm", up_mms)
                                wv("scalar", "mm", gm)
                                if c == 0 and sub == 0:
                                    if n == 0:
                                        wr("scalar", carry["ph"])
                                    else:
                                        wv("scalar", "mm", swdone[n - 1])
                                bcol = (bcols[:, n * 8 + c:n * 8 + c + 1] if n < 6
                                        else bcols[:, 96 + (n - 6) * 8 + c:96 + (n - 6) * 8 + c + 1])
                                eva = incQ("scalar", "ev",
                                           lambda e, c=c, sub=sub, b=b, bcol=bcol: e.activation(
                                               ph[:, c, sub * 512:(sub + 1) * 512], psA[:, b * 512:(b + 1) * 512],
                                               AF.Identity, bias=bcol, scale=1.0))
                                bankrel[b] = ("ev", eva)
                                last_up_ev[sub] = eva
                            up_cons[gidx] = S.c["mm"]
                            issue_up(gidx + 3)
                        updone[n] = S.c["mm"]

                        h2done = {}
                        for c in range(8):
                            gidx = tile_iter * 64 + n * 8 + c
                            wv("tensor", "wld", sw_ld[gidx])
                            for sub in range(2):
                                b = acquire_bank()
                                if c == 0:
                                    wv("tensor", "ev", last_up_ev[sub])

                                def sw_mms(e, c=c, sub=sub, b=b, slot=gidx % 4):
                                    for k in range(ET):
                                        mi = e.matmul(psA[:, b * 512:(b + 1) * 512], swb[:, slot, k, :],
                                                      ph[:, k, sub * 512:(sub + 1) * 512],
                                                      start=(k == 0), stop=(k == ET - 1))
                                    return mi
                                gm = incQ("tensor", "mm", sw_mms)
                                wv("scalar", "mm", gm)
                                if c == 0 and sub == 0 and n > 0:
                                    wv("scalar", "mm", dndone[n - 1])
                                bcol = (bcols[:, 48 + n * 8 + c:48 + n * 8 + c + 1] if n < 6
                                        else bcols[:, 112 + c:112 + c + 1])
                                eva = incQ("scalar", "ev",
                                           lambda e, c=c, sub=sub, b=b, bcol=bcol: e.activation(
                                               ph[:, ET + c, sub * 512:(sub + 1) * 512], psA[:, b * 512:(b + 1) * 512],
                                               AF.Silu, bias=bcol, scale=1.0))
                                bankrel[b] = ("ev", eva)
                                wv("vector", "ev", eva)
                                h2s = ph[:, ET + c, sub * 512:(sub + 1) * 512]
                                if n < 6:
                                    flQ("vector", lambda e, h2s=h2s, c=c, sub=sub: e.tensor_tensor(
                                        h2s, h2s, ph[:, c, sub * 512:(sub + 1) * 512], ALU.mult))
                                    h2done[sub] = incQ("vector", "evv",
                                                       lambda e, h2s=h2s, sub=sub: e.tensor_tensor(
                                                           h2s, h2s, wbn[:, sub * 512:(sub + 1) * 512], ALU.mult))
                                else:
                                    h2done[sub] = incQ("vector", "evv",
                                                       lambda e, h2s=h2s, c=c, sub=sub: e.tensor_tensor(
                                                           h2s, h2s, ph[:, c, sub * 512:(sub + 1) * 512], ALU.mult))
                            sw_cons[gidx] = S.c["mm"]
                            issue_sw(gidx + 4)
                        swdone[n] = S.c["mm"]

                        for m in range(HT):
                            gidx = tile_iter * 128 + n * 16 + m
                            wv("tensor", "dld", dn_ld[gidx])
                            for sub in range(2):
                                b = acquire_bank()
                                if m == 0:
                                    wv("tensor", "evv", h2done[sub])

                                def dn_mms(e, n=n, m=m, sub=sub, b=b, slot=gidx % 4):
                                    for k in range(ET):
                                        mi = e.matmul(psA[:, b * 512:(b + 1) * 512], dnb[:, slot, k, :],
                                                      ph[:, ET + k, sub * 512:(sub + 1) * 512],
                                                      start=(k == 0), stop=(k == ET - 1 and n != 0))
                                    if n == 0:
                                        mi = e.matmul(psA[:, b * 512:(b + 1) * 512],
                                                      bext_bf[0:7, m * 128:(m + 1) * 128],
                                                      wTx[0:7, sub * 512:(sub + 1) * 512], start=False, stop=True)
                                    return mi
                                gm = incQ("tensor", "mm", dn_mms)
                                wv("vector", "mm", gm)
                                accsl = acc[:, m, sub * 512:(sub + 1) * 512]
                                if n == 0:
                                    av = incQ("vector", "evv",
                                              lambda e, accsl=accsl, b=b: e.tensor_copy(accsl, psA[:, b * 512:(b + 1) * 512]))
                                else:
                                    av = incQ("vector", "evv",
                                              lambda e, accsl=accsl, b=b: e.tensor_tensor(accsl, accsl, psA[:, b * 512:(b + 1) * 512], ALU.add))
                                bankrel[b] = ("evv", av)
                            dn_cons[gidx] = S.c["mm"]
                            issue_dn(gidx + 4)
                        dndone[n] = S.c["mm"]

                    if j == 0:
                        zst = {}
                        for m in range(HT):
                            if m >= 2:
                                wv("scalar", "st", zst[m - 2])
                            else:
                                wr("scalar", carry["rb16"][m % 2])
                            dmaQ("scalar", "xld", rb16[:, m % 2, 0], fm(r0hi_s)[:, m, base:base + TT])
                            rld = dmaQ("scalar", "xld", rb16[:, m % 2, 1], fm(r0lo_s)[:, m, base:base + TT])
                            wv("vector", "xld", rld)
                            flQ("vector", lambda e, m=m: e.tensor_tensor(acc[:, m, :], acc[:, m, :], rb16[:, m % 2, 0], ALU.add))
                            flQ("vector", lambda e, m=m: e.tensor_tensor(acc[:, m, :], acc[:, m, :], rb16[:, m % 2, 1], ALU.add))
                            flQ("vector", lambda e, m=m: e.tensor_copy(rb16[:, m % 2, 0], acc[:, m, :]))
                            zev = incQ("vector", "evv",
                                       lambda e, m=m: e.tensor_tensor(rb16[:, m % 2, 1], acc[:, m, :], rb16[:, m % 2, 0], ALU.subtract))
                            wv("gpsimd", "evv", zev)
                            dmaQ("gpsimd", "st", fm(xt1hi_s)[:, m, base:base + TT], rb16[:, m % 2, 0])
                            zst[m] = dmaQ("gpsimd", "st", fm(xt1lo_s)[:, m, base:base + TT], rb16[:, m % 2, 1])
                        zstore[t] = S.c["st"]
                        carry["xtu"] = ("mm", updone[NU - 1])
                        carry["ph"] = None
                        carry["rb16"] = {0: ("st", zst[14]), 1: ("st", zst[15])}
                        carry["psW"] = ("evv", wbnrel[5])
                        carry["psG"] = ("evv", wtx_done)
                        carry["wTx"] = ("mm", selmm[5])
                    else:
                        zrel = {}; smm = {}
                        for m in range(HT):
                            if m >= 2:
                                wv("scalar", "evv", zrel[m - 2])
                            else:
                                wr("scalar", carry["rb16"][m % 2])
                            dmaQ("scalar", "xld", rb16[:, m % 2, 0], fm(r0hi_s)[:, m, base:base + TT])
                            rld = dmaQ("scalar", "xld", rb16[:, m % 2, 1], fm(r0lo_s)[:, m, base:base + TT])
                            wv("vector", "xld", rld)
                            flQ("vector", lambda e, m=m: e.tensor_tensor(acc[:, m, :], acc[:, m, :], rb16[:, m % 2, 0], ALU.add))
                            flQ("vector", lambda e, m=m: e.tensor_tensor(acc[:, m, :], acc[:, m, :], rb16[:, m % 2, 1], ALU.add))
                            flQ("vector", lambda e, m=m: e.tensor_copy(xtu[:, m, :], acc[:, m, :]))
                            zev = incQ("vector", "evv",
                                       lambda e, m=m: e.tensor_tensor(ph[:, m, :], acc[:, m, :], xtu[:, m, :], ALU.subtract))
                            zrel[m] = zev
                            wv("scalar", "evv", zev)
                            if m >= 1:
                                wv("scalar", "mm", smm[m - 1])
                            sqe = incQ("scalar", "ev", lambda e, m=m: e.activation(zsqb[:], xtu[:, m, :], AF.Square))
                            wv("tensor", "evv", zev)
                            wv("tensor", "ev", sqe)
                            if m == 0:
                                wv("tensor", "evv", wbnrel[5])

                            def stat_mms(e, m=m):
                                for sub in range(2):
                                    tgt = psG if sub == 0 else psS
                                    e.matmul(tgt[0:1, 0:512], onescol[:], xtu[:, m, sub * 512:(sub + 1) * 512],
                                             start=(m == 0), stop=False)
                                    e.matmul(tgt[0:1, 0:512], onescol[:], ph[:, m, sub * 512:(sub + 1) * 512],
                                             start=False, stop=(m == HT - 1))
                                e.matmul(psW[0:1, 0:512], onescol[:], zsqb[:, 0:512],
                                         start=(m == 0), stop=(m == HT - 1))
                                return e.matmul(psW[0:1, 512:1024], onescol[:], zsqb[:, 512:1024],
                                                start=(m == 0), stop=(m == HT - 1))
                            smm[m] = incQ("tensor", "mm", stat_mms)

                        Asl = [rrow[0:1, i // 2, (i % 2) * 512:(i % 2) * 512 + 512] for i in range(4)]
                        Bsl = [stg[0:1, i // 2, (i % 2) * 512:(i % 2) * 512 + 512] for i in range(4)]
                        wv("vector", "mm", smm[HT - 1])
                        wr("vector", carry["stg"])
                        rows_done = None
                        for sub in range(2):
                            s1ap = (psG if sub == 0 else psS)[0:1, 0:512]
                            s2ap = psW[0:1, sub * 512:(sub + 1) * 512]
                            MR = Bsl[0] if sub == 0 else Bsl[2]
                            R = Bsl[1] if sub == 0 else Bsl[3]
                            s0, s1_, s2_, s3_, s4_ = Asl[0], Asl[1], Asl[2], Asl[3], Bsl[3]
                            flQ("vector", lambda e, s0=s0, s1ap=s1ap: e.tensor_scalar(s0, s1ap, 1.0 / D, None, ALU.mult))
                            flQ("vector", lambda e, s1_=s1_, s2ap=s2ap: e.tensor_scalar(s1_, s2ap, 1.0 / D, None, ALU.mult))
                            flQ("vector", lambda e, s2_=s2_, s0=s0: e.tensor_mul(s2_, s0, s0))
                            flQ("vector", lambda e, s4_=s4_, s1_=s1_, s2_=s2_: e.tensor_sub(s4_, s1_, s2_))
                            ve = incQ("vector", "evv", lambda e, s3_=s3_, s4_=s4_: e.tensor_scalar_add(s3_, s4_, EPS))
                            wv("scalar", "evv", ve)
                            flQ("scalar", lambda e, s2_=s2_, s3_=s3_: e.activation(s2_, s3_, AF.Ln))
                            se = incQ("scalar", "ev", lambda e, s4_=s4_, s2_=s2_: e.activation(s4_, s2_, AF.Exp, bias=0.0, scale=-0.5))
                            wv("vector", "ev", se)
                            flQ("vector", lambda e, s1_=s1_, s4_=s4_: e.tensor_mul(s1_, s4_, s4_))
                            flQ("vector", lambda e, s2_=s2_, s1_=s1_, s3_=s3_: e.tensor_mul(s2_, s1_, s3_))
                            flQ("vector", lambda e, s1_=s1_, s2_=s2_: e.tensor_scalar(s1_, s2_, -0.5, 1.5, ALU.mult, ALU.add))
                            flQ("vector", lambda e, R=R, s4_=s4_, s1_=s1_: e.tensor_mul(R, s4_, s1_))
                            rows_done = incQ("vector", "evv", lambda e, MR=MR, s0=s0, R=R: e.tensor_mul(MR, s0, R))
                        wv("tensor", "evv", rows_done)

                        def bc_mms(e):
                            e.matmul(psW[:, 0:512], ones1f[:], Bsl[0], start=True, stop=True)
                            e.matmul(psW[:, 512:1024], ones1f[:], Bsl[2], start=True, stop=True)
                            e.matmul(psG[:, 0:512], ones1f[:], Bsl[1], start=True, stop=True)
                            return e.matmul(psS[:, 0:512], ones1f[:], Bsl[3], start=True, stop=True)
                        bc_mm = incQ("tensor", "mm", bc_mms)

                        yrel = {}; ost = {}
                        for m in range(HT):
                            if m < 2:
                                wv("scalar", "evv", rows_done)
                                wr("scalar", carry["rrow"][m % 2])
                            else:
                                wv("scalar", "evv", yrel[m - 2])
                            yld = dmaQ("scalar", "xld", rrow[:, m % 2], fm(y_s)[:, m, base:base + TT])
                            wv("vector", "xld", yld)
                            if m == 0:
                                wv("vector", "mm", bc_mm)
                            if m >= 2:
                                wv("vector", "st", ost[m - 2])
                            orv = None
                            for sub in range(2):
                                Rps = psG if sub == 0 else psS
                                ss = stg[:, m % 2, sub * 512:(sub + 1) * 512]
                                flQ("vector", lambda e, ss=ss, m=m, sub=sub, Rps=Rps: e.tensor_tensor(
                                    ss, acc[:, m, sub * 512:(sub + 1) * 512], Rps[:, 0:512], ALU.mult))
                                flQ("vector", lambda e, ss=ss, sub=sub: e.tensor_tensor(
                                    ss, ss, psW[:, sub * 512:(sub + 1) * 512], ALU.subtract))
                                opl = lambda e, ss=ss, m=m, sub=sub: e.tensor_tensor(
                                    ss, ss, rrow[:, m % 2, sub * 512:(sub + 1) * 512], ALU.add)
                                if sub == 1:
                                    orv = incQ("vector", "evv", opl)
                                else:
                                    flQ("vector", opl)
                            yrel[m] = orv
                            wv("gpsimd", "evv", orv)
                            ost[m] = dmaQ("gpsimd", "st", fm(x2T)[:, m, base:base + TT], stg[:, m % 2])
                        carry["xtu"] = ("mm", smm[HT - 1])
                        carry["ph"] = ("mm", smm[HT - 1])
                        carry["psG"] = ("evv", yrel[15])
                        carry["psW"] = ("evv", yrel[15])
                        carry["rrow"] = {0: ("evv", yrel[14]), 1: ("evv", yrel[15])}
                        carry["rb16"] = {0: ("evv", zrel[14]), 1: ("evv", zrel[15])}
                        carry["stg"] = ("st", ost[15])
                        carry["wTx"] = ("mm", selmm[5])
                    tile_iter += 1

            wv("gpsimd", "st", S.c["st"])

            with nc.Block() as blk:
                @blk.tensor
                def _(te):
                    for fn in Q["tensor"]:
                        fn(te)

                @blk.scalar
                def _(sc):
                    for fn in Q["scalar"]:
                        fn(sc)

                @blk.vector
                def _(v):
                    for fn in Q["vector"]:
                        fn(v)

                @blk.sync
                def _(e):
                    for fn in Q["sync"]:
                        fn(e)

                @blk.gpsimd
                def _(gp):
                    for fn in Q["gpsimd"]:
                        fn(gp)
    return nc


def _consts():
    u = np.arange(128)
    ubits = ((u[:, None] >> np.arange(7)[None, :]) & 1).astype(np.float32)
    c1 = 127 - u
    c1bits = ((c1[:, None] >> np.arange(7)[None, :]) & 1).astype(np.float32)
    s = np.arange(128)[:, None]
    t = np.arange(128)[None, :]
    expn = np.maximum(s - t - 1, 0)
    expnbits = np.concatenate([((expn >> b) & 1).astype(np.float32) for b in range(7)], axis=1).astype(BF)
    causT = (s > t).astype(np.float32).astype(BF)
    sel = np.zeros((8, 768), np.float32)
    for n in range(6):
        sel[n, n * 128 : (n + 1) * 128] = 1.0
    return dict(
        eye_bfd=np.eye(128, dtype=np.float32).astype(BF),
        eye_fd=np.eye(128, dtype=np.float32),
        ones1fd=np.ones((1, 128), np.float32),
        onescold=np.ones((128, 1), np.float32).astype(BF),
        selbfd=sel.astype(BF),
        ubitsd=ubits, c1bitsd=c1bits, expnbitsd=expnbits, causTd=causT,
    )


def _prep_inputs(inputs, T):
    f = lambda k: np.ascontiguousarray(np.asarray(inputs[k], np.float32))
    wg3 = f("L0_attn_Wg")
    wg3h = np.ascontiguousarray(wg3.reshape(16, 128, 3).transpose(1, 0, 2).reshape(128, 48))
    fg = f("L0_ffn_Wg")
    fgh = np.ascontiguousarray(fg.reshape(2, 16, 128, 8).transpose(0, 2, 1, 3).reshape(2, 128, 128))
    bup = f("L0_ffn_bup").reshape(6, 8, 128).transpose(2, 0, 1).reshape(128, 48)
    bsw = f("L0_ffn_bsw").reshape(6, 8, 128).transpose(2, 0, 1).reshape(128, 48)
    bsu = f("L0_ffn_bsu").reshape(2, 8, 128).transpose(2, 0, 1).reshape(128, 16)
    bsa = f("L0_ffn_bsa").reshape(8, 128).transpose(1, 0).reshape(128, 8)
    biascols = np.ascontiguousarray(np.concatenate([bup, bsw, bsu, bsa], axis=1))
    bext = np.ascontiguousarray(np.concatenate([f("L0_ffn_bdn"), f("L0_ffn_bsd").reshape(1, -1)], 0))
    common = dict(
        wq=f("L0_attn_Wq"), wk=f("L0_attn_Wk"), wv=f("L0_attn_Wv"), wo=f("L0_attn_Wo"),
        wg3h=wg3h, dlog=f("L0_attn_decay_logits").reshape(1, 3),
        oscale=np.asarray(inputs["L0_attn_out_scale"], np.float32).reshape(1, 1),
        fgh=fgh, bgh=f("L0_ffn_bg").reshape(1, 16),
        fwup=f("L0_ffn_Wup"), fwsw=f("L0_ffn_Wsw"), fwdn=f("L0_ffn_Wdn"),
        fwsu=f("L0_ffn_Wsu"), fwsa=f("L0_ffn_Wsa"), fwsd=f("L0_ffn_Wsd"),
        biascols=biascols, bexth=bext,
    )
    common.update(_consts())
    x = np.asarray(inputs["L0_attn_x"], np.float32)
    maps = []
    for b in range(x.shape[0]):
        m = dict(common)
        m["xT"] = np.ascontiguousarray(x[b, :T].T)
        maps.append(m)
    return maps


def kernel(**inputs):
    T = 4096
    nc = build_nc(T)
    in_maps = _prep_inputs(inputs, T)
    res = run_bass_kernel_spmd(nc, in_maps, core_ids=list(range(8)))
    out = np.stack([np.ascontiguousarray(np.asarray(r["x2T"]).T) for r in res.results], 0)
    return out.astype(np.float32)
